# version 04
# baseline (speedup 1.0000x reference)
import numpy as np
from contextlib import ExitStack

import concourse.bass as bass
import concourse.mybir as mybir
from concourse.bass_utils import run_bass_kernel_spmd

F32 = mybir.dt.float32
BF16 = mybir.dt.bfloat16
AF = mybir.ActivationFunctionType
ALU = mybir.AluOpType
AX = mybir.AxisListType

NT = 2048
D = 2048
NH = 16
NJ = 16
NTC = 16
NTT = 4
NE = 8
NP = 11
EPS_LN = 1e-5
EPS_ATT = 1e-6


class Prog:

    ENGS = ("tensor", "scalar", "vector", "sync", "gpsimd")

    def __init__(self, nc, es):
        self.nc = nc
        self.es = es
        self.ops = {e: [] for e in self.ENGS}
        self.cnt = {}
        self.hdl = {}

    def sem(self, name):
        if name not in self.hdl:
            self.hdl[name] = self.es.enter_context(self.nc.semaphore(name))
            self.cnt[name] = 0
        return name

    AUTO = {"vector": "V", "scalar": "C"}

    def add(self, eng, fn, waits=(), inc=None, k=1):
        waits = [(s, t) for (s, t) in waits if t > 0]
        for s, t in waits:
            self.sem(s)
            assert t <= self.cnt[s], f"wait on future inc of {s}: {t} > {self.cnt[s]}"
        if inc is None and eng in self.AUTO:
            inc = self.AUTO[eng]
            k = 1
        tgt = None
        if inc is not None:
            self.sem(inc)
            self.cnt[inc] += k
            tgt = self.cnt[inc]
        self.ops[eng].append((fn, waits, inc, k))
        return tgt

    def dma(self, queue, out, in_, waits=(), inc=None):
        def fn(e, out=out, in_=in_):
            return e.dma_start(out=out, in_=in_)
        return self.add(queue, fn, waits=waits, inc=inc, k=16)

    def emit(self):
        nc = self.nc
        hdl = self.hdl
        with nc.Block() as b:
            def run(eng_obj, name):
                auto = self.AUTO.get(name)
                pos = 0
                for fn, waits, inc, k in self.ops[name]:
                    if auto is not None and pos > 0:
                        eng_obj.wait_ge(hdl[auto], pos)
                    for s, t in waits:
                        eng_obj.wait_ge(hdl[s], t)
                    ins = fn(eng_obj)
                    if inc is not None:
                        ins.then_inc(hdl[inc], k)
                    if auto is not None:
                        assert inc == auto and k == 1
                        pos += 1
            b.tensor(lambda t: run(t, "tensor"))
            b.scalar(lambda s: run(s, "scalar"))
            b.vector(lambda v: run(v, "vector"))
            b.sync(lambda s: run(s, "sync"))
            b.gpsimd(lambda g: run(g, "gpsimd"))


def _rope_tables():
    try:
        import jax
        import jax.numpy as jnp
        cpu = jax.devices("cpu")[0]
        with jax.default_device(cpu):
            theta = jnp.asarray(10000.0, jnp.float32) ** (
                -jnp.arange(64, dtype=jnp.float32) * 2.0 / 128)
            pos = jnp.arange(NT, dtype=jnp.float32)
            ang = pos[:, None] * theta[None, :]
            cos = np.asarray(jnp.cos(ang), np.float32)
            sin = np.asarray(jnp.sin(ang), np.float32)
    except Exception:
        theta = np.float32(10000.0) ** (
            -np.arange(64, dtype=np.float32) * np.float32(2.0 / 128))
        ang = (np.arange(NT, dtype=np.float32)[:, None] * theta[None, :]).astype(np.float32)
        cos = np.cos(ang).astype(np.float32)
        sin = np.sin(ang).astype(np.float32)
    return cos.T.copy(), sin.T.copy()


def build_nc(stop_after="full"):
    nc = bass.Bass("TRN2", target_bir_lowering=False)
    dt = nc.dram_tensor
    x_d = dt("x", [NT, D], F32, kind="ExternalInput").ap()
    wq_d = dt("wq", [NH, 128, NJ, 128], F32, kind="ExternalInput").ap()
    wk_d = dt("wk", [NH, 128, NJ, 128], F32, kind="ExternalInput").ap()
    wv_d = dt("wv", [NTT, 128, NJ, 512], F32, kind="ExternalInput").ap()
    wo_d = dt("wo", [NJ, NTT, 128, 512], F32, kind="ExternalInput").ap()
    gw_d = dt("gw", [128, NJ, NE], F32, kind="ExternalInput").ap()
    w1_d = dt("w1", [NE, 22, 128, NJ, 128], F32, kind="ExternalInput").ap()
    w2_d = dt("w2", [NE, NP, NTT, 128, 512], F32, kind="ExternalInput").ap()
    sw1_d = dt("sw1", [NP, 128, NJ, 128], F32, kind="ExternalInput").ap()
    sw2_d = dt("sw2", [NP, 128, NJ, 128], F32, kind="ExternalInput").ap()
    sw3_d = dt("sw3", [NP, NTT, 128, 512], F32, kind="ExternalInput").ap()
    cos_d = dt("cosT", [64, NT], F32, kind="ExternalInput").ap()
    sin_d = dt("sinT", [64, NT], F32, kind="ExternalInput").ap()
    nsin_d = dt("nsinT", [64, NT], F32, kind="ExternalInput").ap()
    mask_d = dt("maskbig", [128, 896], F32, kind="ExternalInput").ap()
    id_d = dt("idf", [128, 128], F32, kind="ExternalInput").ap()
    onesc_d = dt("onesc", [128, 1], BF16, kind="ExternalInput").ap()
    onesr_d = dt("onesr", [1, 128], F32, kind="ExternalInput").ap()
    perm_d = dt("permT", [128, 128], BF16, kind="ExternalInput").ap()
    sel_d = dt("sel", [8, 8 * 128], F32, kind="ExternalInput").ap()
    out_d = dt("out", [NT, D], F32, kind="ExternalOutput").ap()
    vscr_d = dt("vscr", [NTT, NTC, 128, 512], BF16, kind="Internal").ap()
    oscr_d = dt("oscr", [NTT, NH, 128, 512], BF16, kind="Internal").ap()
    cwscr_d = dt("cwscr", [8, NT], F32, kind="Internal").ap()

    es = ExitStack()
    with es:
        sb = nc.sbuf_tensor
        xn = es.enter_context(sb("xn", [128, NJ * NT], BF16))
        acc = es.enter_context(sb("acc", [128, 8 * D], F32))
        af = es.enter_context(sb("af", [128, 7552], F32))
        ab = es.enter_context(sb("ab", [128, 15360], BF16))
        wqsb = es.enter_context(sb("wqsb", [128, NJ * 128], BF16))
        wksb = es.enter_context(sb("wksb", [128, NJ * 128], BF16))
        vh = [es.enter_context(sb(f"vh{i}", [128, NTC * 128], BF16)) for i in range(2)]
        gwsb = es.enter_context(sb("gwsb", [128, NJ * NE], BF16))
        idf = es.enter_context(sb("idf_s", [128, 128], F32))
        onesc = es.enter_context(sb("onesc_s", [128, 1], BF16))
        onesr = es.enter_context(sb("onesr_s", [1, 128], F32))
        permT = es.enter_context(sb("permT_s", [128, 128], BF16))
        st1 = es.enter_context(sb("st1", [128, 12], F32))
        gt1 = es.enter_context(sb("gt1", [128, 40], F32))
        cwc = es.enter_context(sb("cwc", [128, 8], F32))
        cwst = es.enter_context(sb("cwst", [8, 128], F32))
        pB = [es.enter_context(nc.psum_tensor(f"pb{i}", [128, 512], F32)) for i in range(8)]

        def A(c0, w, p0=0, p1=128):
            return af[p0:p1, c0:c0 + w]
        def B(c0, w, p0=0, p1=128):
            return ab[p0:p1, c0:c0 + w]

        p = Prog(nc, es)
        for s in ("T", "V", "C"):
            p.sem(s)
        m = {}

        def dmam(queue, out, in_, sem, waits=()):
            p.dma(queue, out, in_, waits=waits, inc=sem)
            return (sem, p.cnt[sem])

        def MM(psum_ap, lhsT_ap, rhs_ap, start, stop):
            def fn(t, o=psum_ap, l=lhsT_ap, r=rhs_ap, st=start, sp=stop):
                return t.matmul(o, l, r, start=st, stop=sp)
            return fn

        dmam("sync", idf[:], id_d[:, :], "cl")
        dmam("sync", onesc[:], onesc_d[:, :], "cl")
        dmam("sync", onesr[:], onesr_d[:, :], "cl")
        m["consts"] = dmam("sync", permT[:], perm_d[:, :], "cl")
        m["gw"] = dmam("gpsimd", gwsb[:], gw_d.rearrange("p c e -> p (c e)"), "gwl")

        m["xload0"] = dmam("sync", A(0, 2048), x_d[0:128, :], "xl")
        for tc in range(NTC):
            xb = A(0, 2048)
            xnb = A(2048, 2048)
            wl = [m[f"xload{tc}"], m["consts"]]
            if tc >= 1:
                wl.append(("T", m[f"xtr{tc-1}"]))
            p.add("vector", lambda v, xb=xb: v.reduce_sum(st1[:, 0:1], xb, axis=AX.X),
                  waits=[m[f"xload{tc}"]])
            p.add("vector", lambda v, xb=xb, xnb=xnb: v.tensor_mul(xnb, xb, xb), waits=wl)
            p.add("vector", lambda v, xnb=xnb: v.reduce_sum(st1[:, 1:2], xnb, axis=AX.X))
            p.add("vector", lambda v: v.tensor_scalar_mul(st1[:, 2:3], st1[:, 0:1], 1.0 / D))
            p.add("vector", lambda v: v.tensor_scalar_mul(st1[:, 3:4], st1[:, 1:2], 1.0 / D),
                  inc="V")
            mu_done = p.cnt["V"]
            p.add("vector", lambda v: v.tensor_mul(st1[:, 4:5], st1[:, 2:3], st1[:, 2:3]),
                  waits=[("V", mu_done)])
            p.add("vector", lambda v: v.tensor_sub(st1[:, 5:6], st1[:, 3:4], st1[:, 4:5]))
            p.add("vector", lambda v: v.tensor_scalar_add(st1[:, 6:7], st1[:, 5:6], EPS_LN),
                  inc="V")
            varp_done = p.cnt["V"]
            p.add("scalar", lambda s: s.activation(st1[:, 7:8], st1[:, 6:7], AF.Sqrt),
                  waits=[("V", varp_done)], inc="C")
            sd_done = p.cnt["C"]
            p.add("vector", lambda v: v.reciprocal(st1[:, 8:9], st1[:, 7:8]),
                  waits=[("C", sd_done)], inc="V")
            rstd_done = p.cnt["V"]
            p.add("vector", lambda v, xb=xb, xnb=xnb: v.tensor_scalar(
                out=xnb, in0=xb, scalar1=st1[:, 2:3], scalar2=st1[:, 8:9],
                op0=ALU.subtract, op1=ALU.mult), waits=[("V", rstd_done)], inc="V")
            m[f"lnts{tc}"] = p.cnt["V"]
            if tc + 1 < NTC:
                m[f"xload{tc+1}"] = dmam("sync", A(0, 2048),
                      x_d[(tc + 1) * 128:(tc + 2) * 128, :], "xl",
                      waits=[("V", m[f"lnts{tc}"])])
            for j in range(NJ):
                tw = [("V", m[f"lnts{tc}"])]
                gidx = tc * NJ + j
                if gidx >= 2:
                    tw.append(("C", m[f"xtc{gidx-2}"]))
                p.add("tensor", lambda t, j=j, xnb=xnb: t.transpose(
                    pB[j % 2].ap()[:, 0:128], xnb[:, j * 128:(j + 1) * 128], idf[:]),
                    waits=tw, inc="T")
                m[f"xtrmm{gidx}"] = p.cnt["T"]
                p.add("scalar", lambda s, j=j, tc=tc: s.copy(
                    xn[:, j * NT + tc * 128: j * NT + tc * 128 + 128],
                    pB[j % 2].ap()[:, 0:128]),
                    waits=[("T", m[f"xtrmm{gidx}"])], inc="C")
                m[f"xtc{gidx}"] = p.cnt["C"]
            m[f"xtr{tc}"] = m[f"xtrmm{tc*NJ+NJ-1}"]
            m[f"xntc{tc}"] = m[f"xtc{tc*NJ+NJ-1}"]
        m["xn_all"] = m[f"xtc{NTC*NJ-1}"]

        def finish():
            p.add("sync", lambda e: e.nop(),
                  waits=[(s, p.cnt[s]) for s in list(p.cnt) if s not in ("T", "V", "C")])
            p.add("tensor", lambda e: e.nop(), waits=[("T", p.cnt["T"])])
            p.add("vector", lambda e: e.nop())
            p.add("scalar", lambda e: e.nop())
            p.emit()
            return nc
        if stop_after == "A":
            return finish()

        dmam("sync", A(0, 2048, 0, 64), cos_d[:, :], "tbl",
             waits=[("V", m[f"lnts{NTC-1}"])])
        dmam("sync", A(0, 2048, 64, 128), cos_d[:, :], "tbl")
        dmam("sync", A(2048, 2048, 0, 64), nsin_d[:, :], "tbl",
             waits=[("T", m[f"xtr{NTC-1}"])])
        dmam("sync", A(2048, 2048, 64, 128), sin_d[:, :], "tbl")
        m["tables"] = dmam("sync", A(4096, 896), mask_d[:, :], "tbl")
        COS = A(0, 2048)
        SIN = A(2048, 2048)
        MSK = A(4096, 896)

        WV = B(0, NJ * 512)
        m["wv0"] = dmam("gpsimd", WV, wv_d[0].rearrange("p c d -> p (c d)"), "wvl")
        glog_done = {}
        for dt_i in range(NTT):
            for tc in range(NTC):
                bank = pB[tc % 2]
                tw = [m[f"wv{dt_i}"], ("C", m[f"xntc{tc}"])]
                if dt_i == 0 and tc < 2:
                    tw.append(("C", m["xn_all"]))
                prev = dt_i * NTC + tc - 2
                if prev >= 0:
                    tw.append(("C", m[f"vevac{prev}"]))
                for j in range(NJ):
                    p.add("tensor", MM(bank.ap()[:, :],
                                       xn[:, j * NT + tc * 128: j * NT + tc * 128 + 128],
                                       WV[:, j * 512:(j + 1) * 512],
                                       j == 0, j == NJ - 1),
                          waits=(tw if j == 0 else []),
                          inc=("T" if j == NJ - 1 else None))
                m[f"vmm{dt_i},{tc}"] = p.cnt["T"]
                if dt_i == 0:
                    gtw = [m["gw"]]
                    if tc >= 1:
                        gtw.append(("V", glog_done[tc - 1]))
                    for j in range(NJ):
                        p.add("tensor", MM(pB[2].ap()[:, 0:8],
                                           xn[:, j * NT + tc * 128: j * NT + tc * 128 + 128],
                                           gwsb[:, j * NE:(j + 1) * NE],
                                           j == 0, j == NJ - 1),
                              waits=(gtw if j == 0 else []),
                              inc=("T" if j == NJ - 1 else None))
                    m[f"gmm{tc}"] = p.cnt["T"]
                vb = vh[0][:, ((dt_i * NTC + tc) % 2) * 512: ((dt_i * NTC + tc) % 2) * 512 + 512]
                ew = [("T", m[f"vmm{dt_i},{tc}"])]
                if prev >= 0:
                    ew.append(m[f"vstore{prev}"])
                p.add("scalar", lambda s, vb=vb, bank=bank: s.copy(vb, bank.ap()[:, :]),
                      waits=ew, inc="C")
                m[f"vevac{dt_i*NTC+tc}"] = p.cnt["C"]
                m[f"vstore{dt_i*NTC+tc}"] = dmam(
                    "sync", vscr_d[dt_i, tc, :, :], vb, f"vs{(dt_i*NTC+tc)%2}",
                    waits=[("C", m[f"vevac{dt_i*NTC+tc}"])])
                if dt_i == 0:
                    lg = gt1[:, 0:8]
                    p.add("vector", lambda v, lg=lg: v.tensor_copy(lg, pB[2].ap()[:, 0:8]),
                          waits=[("T", m[f"gmm{tc}"])])
                    p.add("vector", lambda v, lg=lg: v.reduce_max(st1[:, 9:10], lg, axis=AX.X),
                          inc="V")
                    m1d = p.cnt["V"]
                    p.add("vector", lambda v, lg=lg: v.tensor_scalar(
                        out=gt1[:, 8:16], in0=lg, scalar1=st1[:, 9:10], scalar2=None,
                        op0=ALU.subtract), waits=[("V", m1d)], inc="V")
                    e1d = p.cnt["V"]
                    p.add("scalar", lambda s: s.activation(gt1[:, 16:24], gt1[:, 8:16], AF.Exp),
                          waits=[("V", e1d)], inc="C")
                    pexp_d = p.cnt["C"]
                    p.add("vector", lambda v, lg=lg: v.tensor_scalar(
                        out=gt1[:, 24:32], in0=lg, scalar1=st1[:, 9:10], scalar2=-1e30,
                        op0=ALU.is_ge, op1=ALU.mult), waits=[("V", m1d)])
                    p.add("vector", lambda v, lg=lg: v.tensor_add(gt1[:, 32:40], lg, gt1[:, 24:32]))
                    p.add("vector", lambda v: v.reduce_max(st1[:, 10:11], gt1[:, 32:40], axis=AX.X),
                          inc="V")
                    m2d = p.cnt["V"]
                    p.add("vector", lambda v, lg=lg: v.tensor_scalar(
                        out=gt1[:, 24:32], in0=lg, scalar1=st1[:, 10:11], scalar2=None,
                        op0=ALU.is_ge), waits=[("V", m2d), ("C", pexp_d)])
                    p.add("vector", lambda v: v.tensor_mul(gt1[:, 8:16], gt1[:, 16:24], gt1[:, 24:32]))
                    p.add("vector", lambda v: v.reduce_sum(st1[:, 11:12], gt1[:, 8:16], axis=AX.X),
                          inc="V")
                    ssum_d = p.cnt["V"]
                    p.add("vector", lambda v: v.reciprocal(st1[:, 9:10], st1[:, 11:12]),
                          waits=[("V", ssum_d)], inc="V")
                    rr_d = p.cnt["V"]
                    p.add("vector", lambda v: v.tensor_scalar(
                        out=cwc[:], in0=gt1[:, 8:16], scalar1=st1[:, 9:10], scalar2=None,
                        op0=ALU.mult), waits=[("V", rr_d)], inc="V")
                    glog_done[tc] = p.cnt["V"]
                    trw = [("V", glog_done[tc])]
                    if tc >= 1:
                        trw.append(("V", m[f"cwT{tc-1}"]))
                    p.add("tensor", lambda t: t.transpose(
                        pB[3].ap()[0:8, 0:128], cwc[:], idf[:]), waits=trw, inc="T")
                    m[f"cwtr{tc}"] = p.cnt["T"]
                    cww = [("T", m[f"cwtr{tc}"])]
                    if tc >= 1:
                        cww.append(m[f"cwst{tc-1}"])
                    p.add("vector", lambda v: v.tensor_copy(
                        cwst[:], pB[3].ap()[0:8, 0:128]), waits=cww, inc="V")
                    m[f"cwT{tc}"] = p.cnt["V"]
                    m[f"cwst{tc}"] = dmam(
                        "sync", cwscr_d[:, tc * 128:(tc + 1) * 128], cwst[:], "cws",
                        waits=[("V", m[f"cwT{tc}"])])
            if dt_i + 1 < NTT:
                m[f"wv{dt_i+1}"] = dmam(
                    "gpsimd", WV, wv_d[dt_i + 1].rearrange("p c d -> p (c d)"), "wvl",
                    waits=[("T", m[f"vmm{dt_i},{NTC-1}"])])
        m["cw_all"] = m[f"cwT{NTC-1}"]
        m["cws_fin"] = ("cws", p.cnt["cws"])
        m["vscr_all_w"] = [("vs0", p.cnt["vs0"]), ("vs1", p.cnt["vs1"])]
        if stop_after == "B":
            return finish()

        S4 = B(0, NTC * 512)
        QH = B(8192, 2048)
        KH = B(10240, 2048)
        OST = [B(12288 + i * 512, 512) for i in range(2)]
        QB = B(13312, 512)
        RT1 = A(4992, 512)
        RT2 = A(5504, 512)
        QBF = A(6016, 512)
        S2F = A(6528, 512)
        RBS = A(7040, 512)
        DSB = af[0:1, 7040:7552]

        dmam("gpsimd", wqsb[:], wq_d[0].rearrange("p c i -> p (c i)"), "qkl",
             waits=[("T", m[f"vmm{NTT-1},{NTC-1}"])])
        m["wqk0"] = dmam("gpsimd", wksb[:], wk_d[0].rearrange("p c i -> p (c i)"), "qkl")
        for q4 in range(4):
            m["vh0"] = dmam(
                "sync", vh[1][:, q4 * 512:(q4 + 1) * 512],
                vscr_d[0, 4 * q4:4 * q4 + 4, :, 0:128].rearrange("c p d -> p c d"),
                "vl1", waits=(m["vscr_all_w"] if q4 == 0 else []))

        gscore = 0
        rope_par = [0]
        rope_guard = [None, None]
        for h in range(NH):
            vbuf = vh[(h + 1) % 2]
            if h + 1 < NH:
                dt_n = (h + 1) // 4
                c0 = ((h + 1) % 4) * 128
                w_ = [("T", m[f"odmm{h-1},{NTT-1}"])] if h >= 1 else [("C", m[f"vevac{NTT*NTC-1}"])]
                for q4 in range(4):
                    m[f"vh{h+1}"] = dmam(
                        "sync", vh[h % 2][:, q4 * 512:(q4 + 1) * 512],
                        vscr_d[dt_n, 4 * q4:4 * q4 + 4, :, c0:c0 + 128].rearrange("c p d -> p c d"),
                        f"vl{h % 2}", waits=(w_ if q4 == 0 else []))
            for tt in range(NTT):
                bq, bk = pB[tt % 2], pB[2 + tt % 2]
                tw = [m[f"wqk{h}"], m["tables"]]
                if h == 0 and tt < 2:
                    tw.append(("V", m["cw_all"]))
                    tw.append(("C", m[f"vevac{NTT*NTC-1}"]))
                if tt >= 2:
                    tw.append(("V", m[f"rope{h},{tt-2}"]))
                elif h >= 1:
                    tw.append(("V", m[f"rope{h-1},{2+tt}"]))
                for j in range(NJ):
                    p.add("tensor", MM(bq.ap()[:, :],
                                       wqsb[:, j * 128:(j + 1) * 128],
                                       xn[:, j * NT + tt * 512: j * NT + (tt + 1) * 512],
                                       j == 0, j == NJ - 1),
                          waits=(tw if j == 0 else []))
                for j in range(NJ):
                    p.add("tensor", MM(bk.ap()[:, :],
                                       wksb[:, j * 128:(j + 1) * 128],
                                       xn[:, j * NT + tt * 512: j * NT + (tt + 1) * 512],
                                       j == 0, j == NJ - 1),
                          inc=("T" if j == NJ - 1 else None))
                m[f"qkmm{h},{tt}"] = p.cnt["T"]
                for (bank, dst) in ((bq, QH), (bk, KH)):
                    par = rope_par[0]
                    rope_par[0] ^= 1
                    swb = pB[4 + par]
                    rope_prev = p.cnt["V"]
                    p.add("scalar", lambda s, bank=bank: s.copy(QBF, bank.ap()[:, :]),
                          waits=[("T", m[f"qkmm{h},{tt}"]), ("V", rope_prev)])
                    p.add("scalar", lambda s: s.copy(QB, QBF), inc="C")
                    msw = p.cnt["C"]
                    sww = [("C", msw)]
                    if rope_guard[par] is not None:
                        sww.append(rope_guard[par])
                    if gscore >= 1:
                        sww.append(("C", m[f"sq1_{gscore-1}"]))
                    p.add("tensor", MM(swb.ap()[:, :], permT[:], QB, True, True),
                          waits=sww, inc="T")
                    mswmm = p.cnt["T"]
                    p.add("scalar", lambda s, swb=swb: s.copy(RT2, swb.ap()[:, :]),
                          waits=[("T", mswmm), ("V", rope_prev)], inc="C")
                    mswf = p.cnt["C"]
                    p.add("vector", lambda v, tt=tt: v.tensor_mul(
                        RT1, QBF, COS[:, tt * 512:(tt + 1) * 512]),
                        waits=[("C", msw)])
                    p.add("vector", lambda v, tt=tt: v.tensor_mul(
                        RT2, RT2, SIN[:, tt * 512:(tt + 1) * 512]),
                        waits=[("C", mswf)])
                    p.add("vector", lambda v, dst=dst, tt=tt: v.tensor_add(
                        dst[:, tt * 512:(tt + 1) * 512], RT1, RT2), inc="V")
                    rope_guard[par] = ("V", p.cnt["V"])
                m[f"rope{h},{tt}"] = p.cnt["V"]
            if h + 1 < NH:
                dmam("gpsimd", wqsb[:], wq_d[h + 1].rearrange("p c i -> p (c i)"), "qkl",
                     waits=[("T", m[f"qkmm{h},{NTT-1}"])])
                m[f"wqk{h+1}"] = dmam("gpsimd", wksb[:],
                                      wk_d[h + 1].rearrange("p c i -> p (c i)"), "qkl")
            for j in range(NTT):
                nk = 4 * j + 4
                for tkc in range(nk):
                    g = gscore
                    gscore += 1
                    sbk = pB[4 + g % 2]
                    tw = [("V", m[f"rope{h},{j}"]), ("V", m[f"rope{h},{tkc//4}"])]
                    if j == 0 and tkc < 2:
                        tw.append(("V", m[f"rope{h},{NTT-1}"]))
                    if g >= 2:
                        tw.append(("C", m[f"sq1_{g-2}"]))
                    p.add("tensor", MM(sbk.ap()[:, :],
                                       KH[:, tkc * 128:(tkc + 1) * 128],
                                       QH[:, j * 512:(j + 1) * 512], True, True),
                          waits=tw, inc="T")
                    m[f"smm{g}"] = p.cnt["T"]
                    sw_ = [("T", m[f"smm{g}"])]
                    if g >= 1:
                        sw_.append(("V", m[f"s4_{g-1}"]))
                    p.add("scalar", lambda s, sbk=sbk: s.activation(S2F, sbk.ap()[:, :], AF.Square),
                          waits=sw_, inc="C")
                    m[f"sq1_{g}"] = p.cnt["C"]
                    vw = [("C", m[f"sq1_{g}"])]
                    if h > 0 or j > 0:
                        pj = j - 1 if j > 0 else NTT - 1
                        phh = h if j > 0 else h - 1
                        vw.append(("T", m[f"odmm{phh},{pj}"]))
                    if tkc >= 4 * j:
                        off = 128 * tkc - 512 * j
                        p.add("vector", lambda v, off=off: v.tensor_mul(
                            S2F, S2F, MSK[:, 384 - off: 384 - off + 512]), waits=vw)
                        p.add("vector", lambda v, tkc=tkc: v.tensor_mul(
                            S4[:, tkc * 512:(tkc + 1) * 512], S2F, S2F), inc="V")
                    else:
                        p.add("vector", lambda v, tkc=tkc: v.tensor_mul(
                            S4[:, tkc * 512:(tkc + 1) * 512], S2F, S2F),
                            waits=vw, inc="V")
                    m[f"s4_{g}"] = p.cnt["V"]
                oq = h * NTT + j
                tw = [("V", m[f"s4_{gscore-1}"]),
                      m["vh0"] if h == 0 else m[f"vh{h}"]]
                if oq >= 1:
                    tw.append(("V", m[f"oev{oq-1}"]))
                for tkc in range(nk):
                    p.add("tensor", MM(pB[6].ap()[:, :],
                                       vbuf[:, tkc * 128:(tkc + 1) * 128],
                                       S4[:, tkc * 512:(tkc + 1) * 512],
                                       tkc == 0, tkc == nk - 1),
                          waits=(tw if tkc == 0 else []))
                for tkc in range(nk):
                    p.add("tensor", MM(pB[7].ap()[0:1, :],
                                       onesc[:],
                                       S4[:, tkc * 512:(tkc + 1) * 512],
                                       tkc == 0, tkc == nk - 1),
                          inc=("T" if tkc == nk - 1 else None))
                m[f"odmm{h},{j}"] = p.cnt["T"]
                p.add("vector", lambda v: v.tensor_copy(DSB, pB[7].ap()[0:1, :]),
                      waits=[("T", m[f"odmm{h},{j}"])])
                p.add("vector", lambda v: v.tensor_scalar_max(DSB, DSB, EPS_ATT))
                p.add("vector", lambda v: v.reciprocal(DSB, DSB), inc="V")
                m[f"rec{oq}"] = p.cnt["V"]
                p.add("tensor", MM(pB[7].ap()[:, :], onesr[:], DSB, True, True),
                      waits=[("V", m[f"rec{oq}"])], inc="T")
                m[f"rbmm{oq}"] = p.cnt["T"]
                p.add("vector", lambda v: v.tensor_copy(RBS, pB[7].ap()[:, :]),
                      waits=[("T", m[f"rbmm{oq}"])])
                p.add("vector", lambda v: v.tensor_copy(RT1, pB[6].ap()[:, :]))
                ow = []
                if oq >= 2:
                    ow.append(m[f"ostore{oq-2}"])
                p.add("vector", lambda v, ob=OST[oq % 2]: v.tensor_mul(
                    ob, RT1, RBS), waits=ow, inc="V")
                m[f"oev{oq}"] = p.cnt["V"]
                m[f"ostore{oq}"] = dmam("sync", oscr_d[j, h, :, :], OST[oq % 2],
                                        f"ost{oq % 2}", waits=[("V", m[f"oev{oq}"])])
        m["oscr_all_w"] = [("ost0", p.cnt["ost0"]), ("ost1", p.cnt["ost1"])]
        m["attn_done_T"] = p.cnt["T"]
        m["attn_done_V"] = p.cnt["V"]
        if stop_after == "C":
            return finish()

        SG = A(0, 512)
        A1 = A(512, 512)
        UF = A(5120, 512)
        EF = [A(5632, 512), A(6144, 512)]
        OUTST = [A(1024 + i * 512, 512) for i in range(2)]
        CWB = A(2048, 1024)
        SEL = af[0:8, 3072:4096]
        CWT = af[0:8, 4096:5120]
        OSB = B(0, NJ * 512)
        ACT = B(0, NP * 1024)
        W1G = [wqsb[:], B(11264, 2048)]
        W1U = [wksb[:], B(13312, 2048)]
        WOM = [vh[0][:, i * 512:(i + 1) * 512] for i in range(4)]
        W2M = [vh[1][:, i * 512:(i + 1) * 512] for i in range(4)]

        m["sel"] = dmam("sync", SEL, sel_d[:, :], "tb2",
                        waits=[("V", m["attn_done_V"])])

        outcnt = 0
        w1_last_read = [m["attn_done_T"]] * 2
        w2_last_read = [m["attn_done_T"]] * 4
        wom_last_read = [m["attn_done_T"]] * 4
        sg_free = m["attn_done_V"]
        ef_free = [m["attn_done_V"], m["attn_done_V"]]
        ef_par = [0]
        devset = [m["attn_done_V"], m["attn_done_V"]]
        for H in range(2):
            cwl = [("V", m["attn_done_V"]), m["cws_fin"]]
            if H == 1:
                cwl.append(("T", m["w2mmlast0,7"]))
            m[f"cwtE{H}"] = dmam("sync", CWT, cwscr_d[:, H * 1024:(H + 1) * 1024],
                                 "cwt", waits=cwl)
            for tci in range(8):
                tc = H * 8 + tci
                w_ = []
                if H == 1:
                    w_.append(("V", m["outev0"]))
                m[f"accxall{H}"] = dmam("sync", acc[:, tci * D:(tci + 1) * D],
                                        x_d[tc * 128:(tc + 1) * 128, :], "xa",
                                        waits=w_)
            for q2 in range(2):
                jq = H * 2 + q2
                if (H, q2) == (0, 0):
                    w_ = [("T", m["attn_done_T"])] + m["oscr_all_w"]
                elif (H, q2) == (1, 0):
                    w_ = [("T", m["w2mmlast0,7"])]
                else:
                    w_ = [("T", m[f"dq_done{H},0"])]
                for c in range(NJ):
                    dmam("sync", OSB[:, c * 512:(c + 1) * 512],
                         oscr_d[jq, c, :, :], "ob", waits=(w_ if c == 0 else []))
                m[f"obq{H},{q2}"] = ("ob", p.cnt["ob"])
                for it in range(NTT):
                    b0 = 4 * (it % 2)
                    for c in range(NJ):
                        gi = it * NJ + c
                        w_ = [("T", wom_last_read[gi % 4])]
                        m[f"wom{H},{q2},{gi}"] = dmam("gpsimd", WOM[gi % 4],
                                                      wo_d[c, it], f"wol{gi % 4}",
                                                      waits=w_)
                        tw = [m[f"wom{H},{q2},{gi}"], m[f"obq{H},{q2}"]]
                        if c == 0:
                            tw.append(("V", devset[it % 2]))
                        for tci4 in range(4):
                            p.add("tensor", MM(pB[b0 + tci4].ap()[:, :],
                                               OSB[:, c * 512 + tci4 * 128: c * 512 + tci4 * 128 + 128],
                                               WOM[gi % 4],
                                               c == 0, c == NJ - 1),
                                  waits=(tw if tci4 == 0 else []),
                                  inc=("T" if tci4 == 3 else None))
                        wom_last_read[gi % 4] = p.cnt["T"]
                    m[f"dit{H},{q2},{it}"] = p.cnt["T"]
                    for tci4 in range(4):
                        tci = q2 * 4 + tci4
                        pe_ = ef_par[0]
                        ef_par[0] ^= 1
                        p.add("scalar", lambda s, b=b0 + tci4, pe_=pe_: s.copy(EF[pe_], pB[b].ap()[:, :]),
                              waits=[("T", m[f"dit{H},{q2},{it}"]), ("V", ef_free[pe_])],
                              inc="C")
                        mef = p.cnt["C"]
                        p.add("vector", lambda v, tci=tci, it=it, pe_=pe_: v.tensor_add(
                            acc[:, tci * D + it * 512: tci * D + (it + 1) * 512],
                            EF[pe_],
                            acc[:, tci * D + it * 512: tci * D + (it + 1) * 512]),
                            waits=[("C", mef), m[f"accxall{H}"]], inc="V")
                        ef_free[pe_] = p.cnt["V"]
                    devset[it % 2] = p.cnt["V"]
                m[f"dq_done{H},{q2}"] = p.cnt["T"]
            m[f"dev{H}"] = p.cnt["V"]

            for ex in range(-1, NE):
                tag = f"{H},{ex}"
                if ex == -1:
                    bank_free = m[f"dev{H}"]
                elif ex == 0:
                    bank_free = m[f"swev{H}"]
                else:
                    bank_free = m[f"wev{H},{ex-1}"]
                if ex >= 0:
                    for tt2 in range(2):
                        p.add("tensor", MM(pB[tt2].ap()[:, :],
                                           SEL[:, ex * 128:(ex + 1) * 128],
                                           CWT[:, tt2 * 512:(tt2 + 1) * 512],
                                           True, True),
                              waits=[("V", bank_free), m[f"cwtE{H}"], m["sel"]],
                              inc="T")
                        m[f"cwb{tag},{tt2}"] = p.cnt["T"]
                    for tt2 in range(2):
                        p.add("vector", lambda v, tt2=tt2: v.tensor_copy(
                            CWB[:, tt2 * 512:(tt2 + 1) * 512], pB[tt2].ap()[:, :]),
                            waits=[("T", m[f"cwb{tag},{tt2}"])], inc="V")
                    m[f"cwbok{tag}"] = p.cnt["V"]
                for pi in range(NP):
                    if ex == -1:
                        srcg, srcu = sw1_d[pi], sw2_d[pi]
                    else:
                        srcg, srcu = w1_d[ex, pi], w1_d[ex, pi + NP]
                    w_ = [("T", w1_last_read[pi % 2])]
                    if H == 0 and ex == -1 and pi == 1:
                        w_ += m["oscr_all_w"]
                    dmam("gpsimd", W1G[pi % 2], srcg.rearrange("p c o -> p (c o)"),
                         f"w1l{pi % 2}", waits=w_)
                    m[f"w1l{tag},{pi}"] = dmam("gpsimd", W1U[pi % 2],
                                               srcu.rearrange("p c o -> p (c o)"),
                                               f"w1l{pi % 2}")
                    tw = [m[f"w1l{tag},{pi}"]]
                    if pi >= 2:
                        tw.append(("V", m[f"actev{tag},{pi-2}"]))
                    elif ex >= 0:
                        tw.append(("V", m[f"cwbok{tag}"]))
                    else:
                        tw.append(("V", bank_free))
                    gb0 = 0 if pi % 2 == 0 else 2
                    ub0 = 4 if pi % 2 == 0 else 6
                    for tt2 in range(2):
                        for j in range(NJ):
                            p.add("tensor", MM(pB[gb0 + tt2].ap()[:, :],
                                               W1G[pi % 2][:, j * 128:(j + 1) * 128],
                                               xn[:, j * NT + H * 1024 + tt2 * 512: j * NT + H * 1024 + (tt2 + 1) * 512],
                                               j == 0, j == NJ - 1),
                                  waits=(tw if (tt2 == 0 and j == 0) else []))
                    for tt2 in range(2):
                        for j in range(NJ):
                            p.add("tensor", MM(pB[ub0 + tt2].ap()[:, :],
                                               W1U[pi % 2][:, j * 128:(j + 1) * 128],
                                               xn[:, j * NT + H * 1024 + tt2 * 512: j * NT + H * 1024 + (tt2 + 1) * 512],
                                               j == 0, j == NJ - 1),
                                  inc=("T" if (tt2 == 1 and j == NJ - 1) else None))
                    m[f"actmm{tag},{pi}"] = p.cnt["T"]
                    w1_last_read[pi % 2] = p.cnt["T"]
                    for tt2 in range(2):
                        p.add("scalar", lambda s, b=gb0 + tt2: s.activation(
                            SG, pB[b].ap()[:, :], AF.Silu),
                            waits=[("T", m[f"actmm{tag},{pi}"]), ("V", sg_free)])
                        p.add("scalar", lambda s, b=ub0 + tt2: s.copy(UF, pB[b].ap()[:, :]),
                              inc="C")
                        msilu = p.cnt["C"]
                        p.add("vector", lambda v: v.tensor_mul(A1, SG, UF),
                              waits=[("C", msilu)])
                        if ex >= 0:
                            p.add("vector", lambda v, pi=pi, tt2=tt2: v.tensor_mul(
                                ACT[:, pi * 1024 + tt2 * 512: pi * 1024 + (tt2 + 1) * 512],
                                A1, CWB[:, tt2 * 512:(tt2 + 1) * 512]), inc="V")
                        else:
                            p.add("vector", lambda v, pi=pi, tt2=tt2: v.tensor_copy(
                                ACT[:, pi * 1024 + tt2 * 512: pi * 1024 + (tt2 + 1) * 512],
                                A1), inc="V")
                        sg_free = p.cnt["V"]
                    m[f"actev{tag},{pi}"] = p.cnt["V"]
                wsrc = sw3_d if ex == -1 else w2_d[ex]
                final = (ex == NE - 1)
                for it in range(NTT):
                    for pi in range(NP):
                        gi = it * NP + pi
                        w_ = [("T", w2_last_read[gi % 4])]
                        m[f"w2l{tag},{gi}"] = dmam("gpsimd", W2M[gi % 4], wsrc[pi, it],
                                                   f"w2m{gi % 4}", waits=w_)
                        tw = [m[f"w2l{tag},{gi}"], ("V", m[f"actev{tag},{pi}"])]
                        if pi == 0:
                            if it >= 1:
                                tw.append(("V", m[f"wevit{tag},{it-1}"]))
                            else:
                                tw.append(("V", m[f"actev{tag},{NP-1}"]))
                        for tci in range(8):
                            p.add("tensor", MM(pB[tci].ap()[:, :],
                                               ACT[:, pi * 1024 + tci * 128: pi * 1024 + tci * 128 + 128],
                                               W2M[gi % 4], pi == 0, pi == NP - 1),
                                  waits=(tw if tci == 0 else []),
                                  inc=("T" if tci == 7 else None))
                        w2_last_read[gi % 4] = p.cnt["T"]
                    m[f"w2mmit{tag},{it}"] = p.cnt["T"]
                    for tci in range(8):
                        ew = [("T", m[f"w2mmit{tag},{it}"])]
                        pe_ = ef_par[0]
                        ef_par[0] ^= 1
                        p.add("scalar", lambda s, tci=tci, pe_=pe_: s.copy(EF[pe_], pB[tci].ap()[:, :]),
                              waits=ew + [("V", ef_free[pe_])], inc="C")
                        mef = p.cnt["C"]
                        ew = [("C", mef)]
                        if final:
                            ob = OUTST[outcnt % 2]
                            if outcnt >= 2:
                                ew.append(m[f"outs{outcnt-2}"])
                            p.add("vector", lambda v, tci=tci, it=it, ob=ob, pe_=pe_: v.tensor_add(
                                ob, EF[pe_],
                                acc[:, tci * D + it * 512: tci * D + (it + 1) * 512]),
                                waits=ew, inc="V")
                            ef_free[pe_] = p.cnt["V"]
                            mv = p.cnt["V"]
                            tcg = H * 8 + tci
                            m[f"outs{outcnt}"] = dmam(
                                "sync",
                                out_d[tcg * 128:(tcg + 1) * 128, it * 512:(it + 1) * 512],
                                ob, f"ou{outcnt % 2}", waits=[("V", mv)])
                            outcnt += 1
                        else:
                            p.add("vector", lambda v, tci=tci, it=it, pe_=pe_: v.tensor_add(
                                acc[:, tci * D + it * 512: tci * D + (it + 1) * 512],
                                EF[pe_],
                                acc[:, tci * D + it * 512: tci * D + (it + 1) * 512]),
                                waits=ew, inc="V")
                            ef_free[pe_] = p.cnt["V"]
                    m[f"wevit{tag},{it}"] = p.cnt["V"]
                m[f"w2mmlast{tag}"] = p.cnt["T"]
                if ex == -1:
                    m[f"swev{H}"] = p.cnt["V"]
                else:
                    m[f"wev{H},{ex}"] = p.cnt["V"]
            m[f"outev{H}"] = p.cnt["V"]
            if stop_after == f"H{H}":
                return finish()

        p.add("sync", lambda e: e.nop(),
              waits=[(s, p.cnt[s]) for s in list(p.cnt) if s not in ("T", "V", "C")])
        p.emit()
    return nc


def _swizzle(inputs):
    Wq = np.asarray(inputs["L0_attn_Wq"], np.float32)
    Wk = np.asarray(inputs["L0_attn_Wk"], np.float32)
    Wv = np.asarray(inputs["L0_attn_Wv"], np.float32)
    Wo = np.asarray(inputs["L0_attn_Wo"], np.float32)
    gw = np.asarray(inputs["L0_ffn_gate_w"], np.float32)
    w1 = np.asarray(inputs["L0_ffn_w1"], np.float32)
    w2 = np.asarray(inputs["L0_ffn_w2"], np.float32)
    sw1 = np.asarray(inputs["L0_ffn_sw1"], np.float32)
    sw2 = np.asarray(inputs["L0_ffn_sw2"], np.float32)
    sw3 = np.asarray(inputs["L0_ffn_sw3"], np.float32)
    C = np.ascontiguousarray
    d = {}
    d["wq"] = C(Wq.T.reshape(16, 128, 16, 128).transpose(2, 1, 0, 3))
    d["wk"] = C(Wk.T.reshape(16, 128, 16, 128).transpose(2, 1, 0, 3))
    d["wv"] = C(Wv.T.reshape(16, 128, 4, 512).transpose(2, 1, 0, 3))
    d["wo"] = C(Wo.T.reshape(16, 128, 4, 512).transpose(0, 2, 1, 3))
    d["gw"] = C(gw.T.reshape(16, 128, 8).transpose(1, 0, 2))
    d["w1"] = C(w1.reshape(8, 16, 128, 22, 128).transpose(0, 3, 2, 1, 4))
    d["w2"] = C(w2.reshape(8, 11, 128, 4, 512).transpose(0, 1, 3, 2, 4))
    d["sw1"] = C(sw1.T.reshape(16, 128, 11, 128).transpose(2, 1, 0, 3))
    d["sw2"] = C(sw2.T.reshape(16, 128, 11, 128).transpose(2, 1, 0, 3))
    d["sw3"] = C(sw3.T.reshape(11, 128, 4, 512).transpose(0, 2, 1, 3))
    cosT, sinT = _rope_tables()
    d["cosT"] = cosT
    d["sinT"] = sinT
    d["nsinT"] = C(-sinT)
    a = np.arange(128, dtype=np.float32)[:, None]
    mcols = np.arange(896, dtype=np.float32)[None, :]
    d["maskbig"] = (mcols >= a + 384).astype(np.float32)
    d["idf"] = np.eye(128, dtype=np.float32)
    import ml_dtypes
    d["onesc"] = np.ones((128, 1), ml_dtypes.bfloat16)
    d["onesr"] = np.ones((1, 128), np.float32)
    import ml_dtypes as _mld
    d["permT"] = np.roll(np.eye(128, dtype=np.float32), 64, axis=0).astype(_mld.bfloat16)
    selm = np.zeros((8, 8 * 128), np.float32)
    for e in range(8):
        selm[e, e * 128:(e + 1) * 128] = 1.0
    d["sel"] = selm
    return d


def kernel(L0_attn_x, L0_attn_Wq, L0_attn_Wk, L0_attn_Wv, L0_attn_Wo,
           L0_ffn_gate_w, L0_ffn_w1, L0_ffn_w2, L0_ffn_sw1, L0_ffn_sw2,
           L0_ffn_sw3):
    x = np.ascontiguousarray(np.asarray(L0_attn_x, np.float32))
    shared = _swizzle(dict(
        L0_attn_Wq=L0_attn_Wq, L0_attn_Wk=L0_attn_Wk, L0_attn_Wv=L0_attn_Wv,
        L0_attn_Wo=L0_attn_Wo, L0_ffn_gate_w=L0_ffn_gate_w, L0_ffn_w1=L0_ffn_w1,
        L0_ffn_w2=L0_ffn_w2, L0_ffn_sw1=L0_ffn_sw1, L0_ffn_sw2=L0_ffn_sw2,
        L0_ffn_sw3=L0_ffn_sw3))
    nc = build_nc()
    in_maps = [dict(shared, x=np.ascontiguousarray(x[c])) for c in range(8)]
    res = run_bass_kernel_spmd(nc, in_maps, core_ids=list(range(8)))
    return np.stack([np.asarray(r["out"], np.float32) for r in res.results], axis=0)
